# version 03
# baseline (speedup 1.0000x reference)
import numpy as np
import ml_dtypes

import concourse.bass as bass
import concourse.mybir as mybir
from concourse.bass_utils import run_bass_kernel_spmd

F32 = mybir.dt.float32
BF16 = mybir.dt.bfloat16
I32 = mybir.dt.int32
AF = mybir.ActivationFunctionType
ALU = mybir.AluOpType

B, S, D, H, HD = 4, 4096, 1024, 16, 64
DFF, NE = 4096, 4
T = 2048
NCH = 8
NTC = 4
NQB = 4
NKC = 32
NPAIR = 8
NM = 32
EPS = 1e-5
NEGM = -9.0e15

STAGE = 5
DBG = False
PARANOID = 0


def ts(i, n=128):
    return slice(i * n, (i + 1) * n)


class Prog:
    def __init__(self, nc):
        self.nc = nc
        self.ops = {k: [] for k in ("t", "s", "v", "g", "y")}
        self.sems = {}
        self.cnt = {}

    def sem(self, name):
        h = self.nc.semaphore(name).__enter__()
        self.sems[name] = h
        self.cnt[name] = 0
        return h

    def emit(self, eng, fn):
        self.ops[eng].append(fn)

    def wait(self, eng, sem, val):
        if val <= 0:
            return
        h = self.sems[sem]
        self.emit(eng, lambda e, h=h, v=val: e.wait_ge(h, v))

    def op(self, eng, fn, inc=None, k=1):
        if inc is None:
            self.emit(eng, fn)
            return None
        h = self.sems[inc]
        self.cnt[inc] += k
        self.emit(eng, lambda e, h=h, fn=fn, k=k: fn(e).then_inc(h, k))
        return self.cnt[inc]

    def dma(self, eng, sem, out, in_):
        return self.op(eng, lambda e, o=out, i=in_: e.dma_start(out=o, in_=i),
                       inc=sem, k=16)


def build_nc():
    nc = bass.Bass("TRN2", target_bir_lowering=False, debug=False,
                   detect_race_conditions=False)

    def din(name, shape, dt=F32):
        return nc.dram_tensor(name, shape, dt, kind="ExternalInput").ap()

    xT = din("xT", [D, T])
    fW = {p: din(f"fW{p}", [NCH, 2, 128, 512]) for p in "qkvo"}
    aW = {p: din(f"aW{p}", [NCH, 2, 128, 512]) for p in "qkvo"}
    fb = {p: din(f"fb{p}", [128, NCH]) for p in "qkvo"}
    ab = {p: din(f"ab{p}", [128, NCH]) for p in "qkvo"}
    w1t = din("w1t", [NE, NM, 2, 128, 512])
    w2r = din("w2r", [NE, DFF, D])
    b1r = din("b1r", [128, 128])
    b2r = din("b2r", [NE, D])
    rw = din("rw", [128, NCH])
    rbias = din("rbias", [1, 1])
    lng = {n: din(n, [128, NCH]) for n in ("ln1g", "ln1b", "ln2g", "ln2b")}
    sI = din("sI", [128, 128], BF16)
    sJ = din("sJ", [128, 128], BF16)
    selin = din("selin", [4, 512], BF16)
    e4in = din("e4in", [4, 1], F32)

    yT = nc.dram_tensor("yT", [D, T], F32, kind="ExternalOutput").ap()
    dbg = nc.dram_tensor("dbgo", [D, T], F32, kind="ExternalOutput").ap() \
        if DBG else None

    xbuf = nc.dram_tensor("xbuf", [D, T], F32).ap()
    ccKi = [nc.dram_tensor(f"ccKi{p}", [128, T], BF16).ap() for p in range(NPAIR)]
    ccKo = [nc.dram_tensor(f"ccKo{p}", [256, T], BF16).ap() for p in range(NPAIR)]
    ccVi = [nc.dram_tensor(f"ccVi{p}", [16, 128, 128], BF16).ap() for p in range(NPAIR)]
    ccVo = [nc.dram_tensor(f"ccVo{p}", [32, 128, 128], BF16).ap() for p in range(NPAIR)]
    hbounce = nc.dram_tensor("hbounce", [NM, NTC, 128, 512], BF16).ap()

    SB0 = nc.alloc_sbuf_tensor("SBBASE", [128, 32], mybir.dt.uint8)
    base = nc.lookup_mloc(SB0).addr + 32
    cur = [base]

    def at(name, shape, dt, align=64):
        nbytes = int(np.prod(shape[1:])) * mybir.dt.size(dt)
        cur[0] = (cur[0] + align - 1) // align * align
        h = nc.alloc_sbuf_tensor_at(name, shape, dt, offset=cur[0])
        cur[0] += nbytes
        return h

    def alias(name, shape, dt, offset):
        return nc.alloc_sbuf_tensor_at(name, shape, dt, offset=offset)

    a1 = cur[0]
    AR1 = at("AR1", [128, 16384], F32)
    W2RES = alias("W2RES", [128, 32768], BF16, a1)
    AR2 = at("AR2", [128, 16384], BF16)
    AR3 = at("AR3", [128, 16384], BF16)
    a4 = cur[0]
    KT0 = at("KT0", [128, 4096], BF16)
    KT1 = at("KT1", [128, 4096], BF16)
    VB0 = at("VB0", [128, 4096], BF16)
    VB1 = at("VB1", [128, 4096], BF16)
    PT0 = at("PT0", [128, 1024], BF16)
    PT1 = at("PT1", [128, 1024], BF16)
    VTMP = at("VTMP", [128, 2048], BF16)
    VTM = at("VTM", [128, 2048], BF16)
    KST = at("KST", [128, 2048], BF16)
    VACC = alias("VACC", [128, 16384], BF16, a4)

    ws0_off = cur[0]
    WS = [at("WS0", [128, 512], F32), at("WS1", [128, 512], F32)]
    wt0_off = cur[0]
    WT = [at("WT0", [128, 1024], BF16), at("WT1", [128, 1024], BF16)]
    hbs_off = cur[0]
    HBS = [at("HBS0", [128, 512], BF16), at("HBS1", [128, 512], BF16)]
    IDENT = at("IDENT", [128, 128], BF16)
    SIB = at("SIB", [128, 128], BF16)
    SJB = at("SJB", [128, 128], BF16)
    CONSTM = at("CONSTM", [128, 512], BF16)
    DIAG = alias("DIAG", [128, 2048], BF16, a1)
    rowa_off = cur[0]
    ROWA = at("ROWA", [128, 4096], F32)
    IROW = alias("IROW", [128, 2048], I32, rowa_off)
    G1 = at("G1", [128, NCH], F32)
    BT1 = at("BT1", [128, NCH], F32)
    G2 = at("G2", [128, NCH], F32)
    BT2 = at("BT2", [128, NCH], F32)
    PBf = {p: at(f"PBf{p}", [128, NCH], F32) for p in "qkvo"}
    PBa = {p: at(f"PBa{p}", [128, NCH], F32) for p in "qkvo"}
    B1S = at("B1S", [128, 128], F32)
    RWS = at("RWS", [128, NCH], F32)
    RB1 = at("RB1", [1, 1], F32)
    SEL = at("SEL", [4, 512], BF16)
    E4 = at("E4", [4, 1], F32)
    ONE1F = at("ONE1F", [128, 128], F32)
    ONE1B = at("ONE1B", [1, 128], BF16)
    ONEC = at("ONEC", [128, 1], BF16)
    vtmp_off = nc.lookup_mloc(VTMP).addr
    vtm_off = nc.lookup_mloc(VTM).addr
    kst_off = nc.lookup_mloc(KST).addr
    pt0_off = nc.lookup_mloc(PT0).addr
    IDX4 = alias("IDX4", [4, 2048], I32, vtmp_off)
    OH4 = alias("OH4", [4, 2048], BF16, kst_off)
    B2E = alias("B2E", [4, 1024], BF16, pt0_off)
    RECB = alias("RECB", [128, 512], F32, hbs_off)
    XCH = [alias("XCH0", [128, 512], F32, vtmp_off),
           alias("XCH1", [128, 512], F32, vtmp_off + 2048)]
    CAST0 = alias("CAST0", [128, 2048], BF16, ws0_off)
    CAST1 = alias("CAST1", [128, 2048], BF16, wt0_off)
    TMPF = alias("TMPF", [128, 512], F32, hbs_off)

    print('SBUF used:', cur[0] - base, 'limit', 229376 - base)
    assert cur[0] - base <= 213000, f"sbuf over budget: {cur[0] - base}"

    PS_A = nc.psum_tensor("PS_A", [128, 1024], F32).__enter__()
    PS_B = nc.psum_tensor("PS_B", [128, 1024], F32).__enter__()
    PS_C = nc.psum_tensor("PS_C", [128, 512], F32).__enter__()
    PS_D = nc.psum_tensor("PS_D", [128, 512], F32).__enter__()
    PS_E = nc.psum_tensor("PS_E", [128, 512], F32).__enter__()
    PS_F = nc.psum_tensor("PS_F", [128, 512], F32).__enter__()
    STPS = [PS_A, PS_B]
    OTPS = [PS_C, PS_D]

    P = Prog(nc)
    for s in ("qy", "qg", "qs", "qv", "t", "s", "v", "g", "qkv0", "qkv1",
              "qhb0", "qhb1", "qxc0", "qxc1",
              "cc0", "cc1", "cc2", "cc3", "cc4", "cc5", "cc6", "cc7"):
        P.sem(s)

    def sl2(buf, c, qc):
        return buf[:, c * 2048 + qc * 512: c * 2048 + (qc + 1) * 512]

    def sum_slot(qc):
        return (PS_E, PS_F, PS_C, PS_D)[qc][0:1, :]

    def sq_slot(qc):
        return (PS_A.ap()[0:1, 0:512], PS_A.ap()[0:1, 512:1024],
                PS_B.ap()[0:1, 0:512], PS_B.ap()[0:1, 512:1024])[qc]

    def phase_consts():
        P.op("g", lambda g: g.memset(IDENT[:], 0.0), inc="g")
        P.wait("g", "g", P.cnt["g"])
        P.op("g", lambda g: g.affine_select(
            out=IDENT[:], in_=IDENT[:], compare_op=ALU.not_equal, fill=1.0,
            base=0, pattern=[[-1, 128]], channel_multiplier=1))
        P.op("g", lambda g: g.memset(CONSTM[:], NEGM))
        P.op("g", lambda g: g.memset(ONE1F[:], 1.0))
        P.op("g", lambda g: g.memset(ONE1B[:], 1.0))
        P.op("g", lambda g: g.memset(ONEC[:], 1.0), inc="g")
        P.dma("s", "qs", SIB[:], sI)
        P.dma("s", "qs", SJB[:], sJ)
        for dst, src in ((G1, lng["ln1g"]), (BT1, lng["ln1b"]),
                         (G2, lng["ln2g"]), (BT2, lng["ln2b"]),
                         (B1S, b1r), (RWS, rw)):
            P.dma("s", "qs", dst[:], src)
        for p in "qkvo":
            P.dma("s", "qs", PBf[p][:], fb[p])
            P.dma("s", "qs", PBa[p][:], ab[p])
        P.dma("s", "qs", RB1[:], rbias)
        P.dma("s", "qs", SEL[:], selin)
        P.dma("s", "qs", E4[:], e4in)
        for e in ("t", "s", "v", "y"):
            P.wait(e, "qs", P.cnt["qs"])
            P.wait(e, "g", P.cnt["g"])
        P.wait("g", "qs", P.cnt["qs"])

    def ln_stats(src, src_is_bf16=False):
        for c in range(NCH):
            if not src_is_bf16:
                P.wait("s", "t", P.cnt["t"])
                P.op("s", lambda s, c=c: s.activation(
                    out=CAST0[:], in_=src[:, ts(c, 2048)], func=AF.Copy),
                     inc="s")
                P.wait("t", "s", P.cnt["s"])
            for qc in range(NTC):
                first, last = (c == 0), (c == NCH - 1)
                s_ap = sl2(src, c, qc) if src_is_bf16 else CAST0[:, ts(qc, 512)]
                P.op("t", lambda t, qc=qc, f=first, l=last, s_ap=s_ap: t.matmul(
                    sum_slot(qc), ONEC[:], s_ap, start=f, stop=l),
                     inc="t" if qc == NTC - 1 else None)
        P.wait("v", "t", P.cnt["t"])
        for qc in range(NTC):
            mu = ROWA[0:1, ts(qc, 512)]
            P.op("v", lambda v, qc=qc, mu=mu: v.tensor_scalar(
                out=mu, in0=sum_slot(qc), scalar1=1.0 / D,
                scalar2=None, op0=ALU.mult), inc="v" if qc == NTC - 1 else None)
        P.wait("t", "v", P.cnt["v"])
        for c in range(NCH):
            P.wait("s", "t", P.cnt["t"])
            P.op("s", lambda s, c=c: s.activation(
                out=CAST1[:], in_=src[:, ts(c, 2048)], func=AF.Square), inc="s")
            P.wait("t", "s", P.cnt["s"])
            for qc in range(NTC):
                first, last = (c == 0), (c == NCH - 1)
                P.op("t", lambda t, qc=qc, f=first, l=last: t.matmul(
                    sum_slot(qc), ONEC[:], CAST1[:, ts(qc, 512)],
                    start=f, stop=l),
                     inc="t" if qc == NTC - 1 else None)
        P.wait("v", "t", P.cnt["t"])
        for qc in range(NTC):
            mu = ROWA[0:1, ts(qc, 512)]
            rs = ROWA[0:1, 2048 + qc * 512: 2048 + (qc + 1) * 512]
            mu2 = TMPF[0:1, :]
            P.op("v", lambda v, qc=qc, rs=rs: v.tensor_scalar(
                out=rs, in0=sum_slot(qc), scalar1=1.0 / D,
                scalar2=None, op0=ALU.mult))
            P.op("v", lambda v, mu=mu, mu2=mu2: v.tensor_mul(out=mu2, in0=mu, in1=mu))
            P.op("v", lambda v, rs=rs, mu2=mu2: v.tensor_sub(
                out=rs, in0=rs, in1=mu2))
            P.op("v", lambda v, rs=rs: v.tensor_scalar(
                out=rs, in0=rs, scalar1=EPS, scalar2=None, op0=ALU.add))
            P.op("v", lambda v, rs=rs: v.reciprocal(out=rs, in_=rs), inc="v")
        P.wait("s", "v", P.cnt["v"])
        for qc in range(NTC):
            rs = ROWA[0:1, 2048 + qc * 512: 2048 + (qc + 1) * 512]
            P.op("s", lambda s, rs=rs: s.activation(
                out=rs, in_=rs, func=AF.Sqrt), inc="s")
        P.wait("t", "s", P.cnt["s"])

    def ln_bcast(qc):
        P.op("t", lambda t, qc=qc: t.matmul(
            PS_E[:], ONE1F[0:1, :], ROWA[0:1, ts(qc, 512)],
            start=True, stop=True))
        P.op("t", lambda t, qc=qc: t.matmul(
            PS_F[:], ONE1F[0:1, :],
            ROWA[0:1, 2048 + qc * 512: 2048 + (qc + 1) * 512],
            start=True, stop=True), inc="t")

    def ln_apply(dst32, dst16, src, gamma, beta, src_is_bf16=False):
        for qc in range(NTC):
            ln_bcast(qc)
            P.wait("v", "t", P.cnt["t"])
            for c in range(NCH):
                P.wait("v", "s", P.cnt["s"])
                P.op("v", lambda v, c=c, qc=qc: v.tensor_sub(
                    out=TMPF[:], in0=sl2(src, c, qc), in1=PS_E[:]))
                t32 = sl2(dst32, c, qc) if dst32 is not None else None
                t16 = sl2(dst16, c, qc) if dst16 is not None else None
                if gamma is None:
                    if t32 is not None and t16 is not None:
                        P.op("v", lambda v: v.tensor_mul(
                            out=TMPF[:], in0=TMPF[:], in1=PS_F[:]))
                        P.op("v", lambda v, t=t32: v.tensor_copy(
                            out=t, in_=TMPF[:]), inc="v")
                        P.wait("s", "v", P.cnt["v"])
                        P.op("s", lambda s, t=t16: s.activation(
                            out=t, in_=TMPF[:], func=AF.Copy), inc="s")
                    elif t16 is not None:
                        P.op("v", lambda v, t=t16: v.tensor_mul(
                            out=t, in0=TMPF[:], in1=PS_F[:]), inc="v")
                    else:
                        P.op("v", lambda v, t=t32: v.tensor_mul(
                            out=t, in0=TMPF[:], in1=PS_F[:]), inc="v")
                else:
                    P.op("v", lambda v: v.tensor_mul(
                        out=TMPF[:], in0=TMPF[:], in1=PS_F[:]), inc="v")
                    P.wait("s", "v", P.cnt["v"])
                    gm, bt = gamma[:, c:c + 1], beta[:, c:c + 1]
                    if t32 is not None:
                        P.op("s", lambda s, t=t32, gm=gm, bt=bt: s.activation(
                            out=t, in_=TMPF[:], func=AF.Identity, scale=gm,
                            bias=bt), inc="s")
                    if t16 is not None:
                        P.op("s", lambda s, t=t16, gm=gm, bt=bt: s.activation(
                            out=t, in_=TMPF[:], func=AF.Identity, scale=gm,
                            bias=bt), inc="s")
            P.wait("t", "v", P.cnt["v"])
            P.wait("t", "s", P.cnt["s"])

    def projection(Wt, rhs, evac):
        psr = [PS_A.ap()[:, 0:512], PS_A.ap()[:, 512:1024],
               PS_B.ap()[:, 0:512], PS_B.ap()[:, 512:1024]]
        t_entry = P.cnt["t"]
        t_after_m = {}
        for m in range(NCH):
            wt = WT[m % 2]
            for half in range(2):
                ws = WS[half]
                P.wait("y", "v", P.cnt["v"])
                P.dma("y", "qy", ws[:], Wt[m, half])
                P.wait("v", "qy", P.cnt["qy"])
                P.wait("v", "t", t_after_m[m - 2] if m >= 2 else t_entry)
                P.op("v", lambda v, wt=wt, ws=ws, h=half: v.tensor_copy(
                    out=wt[:, ts(h, 512)], in_=ws[:]), inc="v")
            P.wait("t", "v", P.cnt["v"])
            for tc in range(NTC):
                ps = psr[tc]
                for k in range(NCH):
                    P.op("t", lambda t, k=k, tc=tc, wt=wt, ps=ps: t.matmul(
                        ps, wt[:, ts(k)], sl2(rhs, k, tc),
                        start=(k == 0), stop=(k == NCH - 1)),
                         inc="t" if k == NCH - 1 else None)
                evac(m, tc, ps)
            t_after_m[m] = P.cnt["t"]
            if PARANOID == 1:
                for e in ("t", "s", "v", "g", "y"):
                    for sm in ("t", "s", "v", "qy", "qg", "qs", "qv"):
                        P.wait(e, sm, P.cnt[sm])
            elif PARANOID == 2:
                for sm in ("t", "s", "v"):
                    P.wait("y", sm, P.cnt[sm])
            elif PARANOID == 3:
                for sm in ("qy", "qg", "qs", "qv"):
                    P.wait("t", sm, P.cnt[sm])
            elif PARANOID == 4:
                for sm in ("t", "s", "v", "qy"):
                    P.wait("g", sm, P.cnt[sm])
            elif PARANOID == 5:
                for sm in ("qy", "qg"):
                    P.wait("s", sm, P.cnt[sm])

    state = {"aid": 0}

    def attention(Wt, bias, u_dst, causal, x_from=None, x_to=None, blk="f"):
        aid = state["aid"]
        state["aid"] += 1
        sc = 1.0 / np.sqrt(HD)

        def evac_q(m, tc, ps):
            P.wait("s", "t", P.cnt["t"])
            P.op("s", lambda s, m=m, tc=tc, ps=ps: s.activation(
                out=sl2(AR3, m, tc), in_=ps, func=AF.Identity,
                bias=bias["q"][:, m:m + 1]), inc="s")
            P.wait("t", "s", P.cnt["s"])

        def evac_k(m, tc, ps):
            P.wait("s", "t", P.cnt["t"])
            P.wait("s", "qg", P.cnt["qg"])
            P.op("s", lambda s, tc=tc, ps=ps, m=m: s.activation(
                out=KST[:, ts(tc, 512)], in_=ps, func=AF.Identity,
                bias=bias["k"][:, m:m + 1]), inc="s")
            P.wait("g", "s", P.cnt["s"])
            P.dma("g", "qg", ccKi[m][:, ts(tc, 512)], KST[:, ts(tc, 512)])
            P.wait("t", "s", P.cnt["s"])

        def evac_v(m, tc, ps):
            P.wait("s", "t", P.cnt["t"])
            P.wait("s", "qy", P.cnt["qy"])
            P.op("s", lambda s, tc=tc, ps=ps, m=m: s.activation(
                out=VTMP[:, ts(tc, 512)], in_=ps, func=AF.Identity,
                bias=bias["v"][:, m:m + 1]), inc="s")
            P.wait("y", "s", P.cnt["s"])
            P.wait("y", "qg", P.cnt["qg"])
            for i in range(4):
                kt = tc * 4 + i
                P.op("y", lambda y, tc=tc, i=i, kt=kt: y.dma_start_transpose(
                    out=VTM[:, ts(kt % 16)],
                    in_=VTMP[:, tc * 512 + i * 128: tc * 512 + (i + 1) * 128]),
                     inc="qy", k=16)
            P.wait("g", "qy", P.cnt["qy"])
            for i in range(4):
                kt = tc * 4 + i
                P.dma("g", "qg", ccVi[m][kt], VTM[:, ts(kt % 16)])
            P.wait("t", "s", P.cnt["s"])

        projection(Wt["q"], AR2, evac_q)
        projection(Wt["k"], AR2, evac_k)
        projection(Wt["v"], AR2, evac_v)
        if DBG == "q":
            state["stop"] = True
            return

        P.wait("g", "qg", P.cnt["qg"])
        if DBG == "cki":
            P.dma("g", "qkv0", KT0[:, 0:2048], ccKi[1])
            P.wait("v", "qkv0", P.cnt["qkv0"])
            P.op("v", lambda v: v.tensor_copy(out=AR1[:, 0:2048],
                                              in_=KT0[:, 0:2048]), inc="v")
            state["stop"] = True
            return
        P.wait("g", "qkv0", P.cnt["qkv0"])
        P.wait("g", "qkv1", P.cnt["qkv1"])
        rg = [[0, 1], [2, 3], [4, 5], [6, 7]]
        for p in range(NPAIR):
            P.op("g", lambda g, p=p: g.collective_compute(
                "AllGather", ALU.bypass, replica_groups=rg,
                ins=[ccKi[p]], outs=[ccKo[p]]), inc=f"cc{p}")
            P.op("g", lambda g, p=p: g.collective_compute(
                "AllGather", ALU.bypass, replica_groups=rg,
                ins=[ccVi[p]], outs=[ccVo[p]]), inc=f"cc{p}")

        t_after_pair = {}
        git = [0]
        for p in range(NPAIR):
            KT = (KT0, KT1)[p % 2]
            VB = (VB0, VB1)[p % 2]
            P.wait("g", f"cc{p}", P.cnt[f"cc{p}"])
            if p >= 2:
                P.wait("g", "t", t_after_pair[p - 2])
            kvs = f"qkv{p % 2}"
            P.dma("g", kvs, KT[:, 0:2048], ccKo[p][0:128, :])
            P.dma("g", kvs, KT[:, 2048:4096], ccKo[p][128:256, :])
            vbv = VB.ap().rearrange("p (t f) -> p t f", f=128)
            srcv = ccVo[p].rearrange("t p f -> p t f")
            P.dma("g", kvs, vbv, srcv)
            need_kv = P.cnt[kvs]

            for qb in range(NQB):
                for kc in range(NKC):
                    it = qb * NKC + kc
                    stp = STPS[git[0] % 2]
                    if git[0] >= 2 or state["aid"] > 1:
                        P.wait("t", "s", P.cnt["s"])
                    git[0] += 1
                    if it == 0:
                        P.wait("t", kvs, need_kv)
                    for hh in range(2):
                        lhs = KT[64 * hh:64 * (hh + 1), ts(kc)]
                        rhsq = AR3[64 * hh:64 * (hh + 1) if hh == 0 else 128,
                                   p * 2048 + qb * 512: p * 2048 + (qb + 1) * 512]
                        psl = stp.ap()[:, 512 * hh: 512 * (hh + 1)]
                        mask_mms = []
                        if causal:
                            dd0 = kc - 4 * qb
                            dd1 = kc - 16 - 4 * qb
                            if kc < 16:
                                if 0 <= dd0 < 4:
                                    mask_mms = [(SIB, DIAG[:, ts(dd0, 512)])]
                                elif dd0 >= 4:
                                    mask_mms = [(SIB, CONSTM[:])]
                            else:
                                if dd1 < 0:
                                    mask_mms = [(SIB, CONSTM[:])]
                                elif dd1 < 4:
                                    mask_mms = [(SJB, DIAG[:, ts(dd1, 512)]),
                                                (SIB, CONSTM[:])]
                                else:
                                    mask_mms = [(IDENT, CONSTM[:])]
                        one_mm = not mask_mms
                        P.op("t", lambda t, l=lhs, r=rhsq, ps=psl, hh=hh,
                             om=one_mm: t.matmul(
                                 ps, l, r, start=True, stop=om,
                                 tile_position=(64 * hh, 0)))
                        for j, (ml, mr) in enumerate(mask_mms):
                            lastm = j == len(mask_mms) - 1
                            P.op("t", lambda t, ps=psl, ml=ml, mr=mr,
                                 lm=lastm: t.matmul(ps, ml[:], mr,
                                                    start=False, stop=lm))
                    last = P.ops["t"].pop()
                    P.cnt["t"] += 1
                    h = P.sems["t"]
                    P.emit("t", lambda t, f=last, h=h: f(t).then_inc(h, 1))
                    tdone = P.cnt["t"]
                    P.wait("s", "t", tdone)
                    PT = (PT0, PT1)[(git[0] - 1) % 2]
                    P.op("s", lambda s, stp=stp, PT=PT: s.activation(
                        out=PT[:], in_=stp.ap()[:, 0:1024], func=AF.Exp,
                        scale=sc), inc="s")
                    P.wait("t", "s", P.cnt["s"])
                    for hh in range(2):
                        lhsv = VB[:, kc * 128 + hh * 64: kc * 128 + (hh + 1) * 64]
                        rr = PT.ap()[:, 512 * hh: 512 * (hh + 1)]
                        P.op("t", lambda t, l=lhsv, r=rr, kc=kc, ot=OTPS[hh]:
                             t.matmul(ot[0:64, :], l, r, start=(kc == 0),
                                      stop=(kc == NKC - 1)))
                        sxp = (PS_E, PS_F)[hh]
                        P.op("t", lambda t, r=rr, kc=kc, sxp=sxp:
                             t.matmul(sxp[64:65, :], ONEC[:], r,
                                      start=(kc == 0), stop=(kc == NKC - 1),
                                      tile_position=(0, 64)),
                             inc="t" if (hh == 1 and kc == NKC - 1) else None)
                P.wait("v", "t", P.cnt["t"])
                r0 = ROWA[0:1, 0:512]
                r1 = ROWA[0:1, 512:1024]
                P.op("v", lambda v, r0=r0: v.reciprocal(out=r0, in_=PS_E[64:65, :]))
                P.op("v", lambda v, r1=r1: v.reciprocal(out=r1, in_=PS_F[64:65, :]),
                     inc="v")
                P.wait("t", "v", P.cnt["v"])
                P.op("t", lambda t, r0=r0: t.matmul(
                    PS_E[:], ONE1F[0:1, :], r0, start=True, stop=True))
                P.op("t", lambda t, r1=r1: t.matmul(
                    PS_F[:], ONE1F[0:1, :], r1, start=True, stop=True),
                     inc="t")
                P.wait("s", "t", P.cnt["t"])
                P.op("s", lambda s: s.copy(out=RECB[0:64, :], in_=PS_C[0:64, :]))
                P.op("s", lambda s: s.copy(out=RECB[64:128, :], in_=PS_D[0:64, :]),
                     inc="s")
                P.wait("v", "s", P.cnt["s"])
                dst = AR2[:, p * 2048 + qb * 512: p * 2048 + (qb + 1) * 512]
                P.op("v", lambda v, dst=dst: v.tensor_mul(
                    out=dst[0:64, :], in0=RECB[0:64, :], in1=PS_E[0:64, :]))
                P.op("v", lambda v, dst=dst: v.tensor_mul(
                    out=dst[64:128, :], in0=RECB[64:128, :], in1=PS_F[64:128, :]),
                     inc="v")
                P.wait("t", "v", P.cnt["v"])
            t_after_pair[p] = P.cnt["t"]
            if DBG == "kv1" and p == 1:
                P.wait("v", "qkv1", P.cnt["qkv1"])
                P.op("v", lambda v: v.tensor_copy(out=AR1[:, 0:4096], in_=KT1[:]))
                P.op("v", lambda v: v.tensor_copy(out=AR1[:, 4096:8192],
                                                  in_=VB1[:]), inc="v")
                state["stop"] = True
                return
            if DBG == "kv" and p == 0:
                P.wait("v", "qkv0", P.cnt["qkv0"])
                P.op("v", lambda v: v.tensor_copy(out=AR1[:, 0:4096], in_=KT0[:]))
                P.op("v", lambda v: v.tensor_copy(out=AR1[:, 4096:8192],
                                                  in_=VB0[:]), inc="v")
                state["stop"] = True
                return
        if DBG == "ot":
            P.wait("v", "t", P.cnt["t"])
            for c in range(NCH):
                P.op("v", lambda v, c=c: v.tensor_copy(
                    out=AR1[:, ts(c, 2048)], in_=AR2[:, ts(c, 2048)]),
                     inc="v" if c == NCH - 1 else None)
            state["stop"] = True
            return

        def evac_o(m, tc, ps):
            P.wait("v", "t", P.cnt["t"])
            if blk == "f":
                dstc = sl2(u_dst, m, tc)
                P.op("v", lambda v, d=dstc, ps=ps, m=m: v.scalar_tensor_tensor(
                    out=d, in0=ps, scalar=bias["o"][:, m:m + 1], in1=d,
                    op0=ALU.add, op1=ALU.add), inc="v")
            else:
                i = m * NTC + tc
                xc = XCH[i % 2]
                P.wait("s", f"qxc{i % 2}", P.cnt[f"qxc{i % 2}"])
                P.dma("s", "qv", xc[:], x_from[ts(m), ts(tc, 512)])
                P.wait("v", "qv", P.cnt["qv"])
                P.op("v", lambda v, xc=xc, ps=ps, m=m: v.scalar_tensor_tensor(
                    out=xc[:], in0=ps, scalar=bias["o"][:, m:m + 1], in1=xc[:],
                    op0=ALU.add, op1=ALU.add), inc="v")
                P.wait("y", "v", P.cnt["v"])
                P.dma("y", f"qxc{i % 2}", x_to[ts(m), ts(tc, 512)], xc[:])
            P.wait("t", "v", P.cnt["v"])

        projection(Wt["o"], AR2, evac_o)

    def moe():
        P.wait("t", "s", P.cnt["s"])
        for qc in range(NTC):
            for k in range(NCH):
                P.op("t", lambda t, qc=qc, k=k: t.matmul(
                    sum_slot(qc), RWS[:, k:k + 1], sl2(AR1, k, qc),
                    start=(k == 0), stop=(k == NCH - 1)),
                     inc="t" if k == NCH - 1 else None)
        P.wait("v", "t", P.cnt["t"])
        for qc in range(NTC):
            P.op("v", lambda v, qc=qc: v.tensor_copy(
                out=ROWA[0:1, ts(qc, 512)], in_=sum_slot(qc)))
        krow = ROWA[0:1, 0:2048]
        P.op("v", lambda v, krow=krow: v.tensor_scalar(
            out=krow, in0=krow, scalar1=RB1[0:1, 0:1], scalar2=None,
            op0=ALU.add))
        irow = IDX4[0:1, :]
        frow = ROWA[0:1, 2048:4096]
        adj = OH4[0:1, :]
        P.op("v", lambda v, krow=krow, irow=irow: v.tensor_copy(
            out=irow, in_=krow))
        P.op("v", lambda v, irow=irow, frow=frow: v.tensor_copy(
            out=frow, in_=irow))
        P.op("v", lambda v, krow=krow, frow=frow, adj=adj: v.tensor_tensor(
            out=adj, in0=frow, in1=krow, op=ALU.is_gt))
        P.op("v", lambda v, frow=frow, adj=adj: v.tensor_tensor(
            out=frow, in0=frow, in1=adj, op=ALU.subtract))
        P.op("v", lambda v, irow=irow, frow=frow: v.tensor_copy(
            out=irow, in_=frow))
        P.op("v", lambda v, irow=irow: v.tensor_scalar(
            out=irow, in0=irow, scalar1=3,
            scalar2=None, op0=ALU.bitwise_and), inc="v")
        P.wait("g", "v", P.cnt["v"])
        for e in range(1, NE):
            P.dma("g", "qg", IDX4[e:e + 1, :], IDX4[0:1, :])
        P.wait("v", "qg", P.cnt["qg"])
        P.op("v", lambda v: v.tensor_scalar(
            out=OH4[:], in0=IDX4[:], scalar1=E4[:, 0:1], scalar2=None,
            op0=ALU.is_equal), inc="v")
        P.wait("g", "v", P.cnt["v"])
        P.dma("g", "qg", B2E[:], b2r)

        for c in range(NCH):
            P.op("s", lambda s, c=c: s.activation(
                out=AR2[:, ts(c, 2048)], in_=AR1[:, ts(c, 2048)],
                func=AF.Copy), inc="s" if c == NCH - 1 else None)
        P.wait("v", "s", P.cnt["s"])
        for c in range(NCH):
            P.op("v", lambda v, c=c: v.tensor_copy(
                out=VACC[:, ts(c, 2048)], in_=AR2[:, ts(c, 2048)]),
                 inc="v" if c == NCH - 1 else None)
        vacc_ready = P.cnt["v"]
        xn_ready = P.cnt["s"]

        t_moe_entry = P.cnt["t"]
        for e in range(NE):
            P.wait("v", "t", t_moe_entry)
            for m in range(NM):
                for hh in range(2):
                    ws = WS[hh]
                    P.wait("y", "v", P.cnt["v"])
                    P.dma("y", "qy", ws[:], w2r[e, ts(m), ts(hh, 512)])
                    P.wait("v", "qy", P.cnt["qy"])
                    P.op("v", lambda v, m=m, hh=hh, ws=ws: v.tensor_copy(
                        out=W2RES[:, m * 1024 + hh * 512: m * 1024 + (hh + 1) * 512],
                        in_=ws[:]), inc="v")
            for qc in range(NTC):
                P.wait("t", "v", P.cnt["v"])
                P.op("t", lambda t, e=e, qc=qc: t.matmul(
                    PS_E[:], SEL[:, ts(e)], OH4[:, ts(qc, 512)],
                    start=True, stop=True), inc="t")
                P.wait("v", "t", P.cnt["t"])
                for k in range(NCH):
                    P.op("v", lambda v, k=k, qc=qc: v.tensor_mul(
                        out=sl2(AR3, k, qc), in0=sl2(AR2, k, qc), in1=PS_E[:]),
                         inc="v" if k == NCH - 1 else None)
            P.wait("t", "v", P.cnt["v"])

            t_p1entry = P.cnt["t"]
            for m in range(NM):
                wt = WT[m % 2]
                for hh in range(2):
                    ws = WS[hh]
                    P.wait("y", "v", P.cnt["v"])
                    P.dma("y", "qy", ws[:], w1t[e, m, hh])
                    P.wait("v", "qy", P.cnt["qy"])
                    P.wait("v", "t", t_w1[m - 2] if m >= 2 else t_p1entry)
                    P.op("v", lambda v, wt=wt, ws=ws, hh=hh: v.tensor_copy(
                        out=wt[:, ts(hh, 512)], in_=ws[:]), inc="v")
                P.wait("t", "v", P.cnt["v"])
                for tc in range(NTC):
                    hps = OTPS[tc % 2]
                    P.wait("t", "s", P.cnt["s"])
                    for k in range(NCH):
                        P.op("t", lambda t, k=k, tc=tc, wt=wt, hps=hps: t.matmul(
                            hps[:], wt[:, ts(k)], sl2(AR3, k, tc),
                            start=(k == 0), stop=(k == NCH - 1)),
                             inc="t" if k == NCH - 1 else None)
                    P.wait("s", "t", P.cnt["t"])
                    P.wait("s", f"qhb{tc % 2}", P.cnt[f"qhb{tc % 2}"])
                    hst = HBS[tc % 2]
                    b1c = B1S[:, e * NM + m: e * NM + m + 1]
                    P.op("s", lambda s, hps=hps, hst=hst, b1c=b1c: s.activation(
                        out=hst[:], in_=hps[:], func=AF.Relu, bias=b1c),
                         inc="s")
                    P.wait("g", "s", P.cnt["s"])
                    P.dma("g", f"qhb{tc % 2}", hbounce[m, tc], hst[:])
                t_w1[m] = P.cnt["t"]

            P.wait("g", "qhb0", P.cnt["qhb0"])
            P.wait("g", "qhb1", P.cnt["qhb1"])
            for tc in range(NTC):
                P.wait("g", "t", P.cnt["t"])
                src = hbounce[:, tc].rearrange("m p n -> p m n")
                dst = AR3.ap().rearrange("p (m n) -> p m n", n=512)[:, 0:NM]
                P.dma("g", "qg", dst, src)
                P.wait("t", "qg", P.cnt["qg"])
                for dm in range(NCH):
                    yps = PS_A.ap()[:, 0:512] if dm % 2 == 0 \
                        else PS_A.ap()[:, 512:1024]
                    if dm >= 2:
                        P.wait("t", "v", P.cnt["v"])
                    if e == 0:
                        P.op("t", lambda t, dm=dm, tc=tc, yps=yps: t.matmul(
                            yps, B2E[:, ts(dm)], OH4[:, ts(tc, 512)],
                            start=True, stop=False))
                    for m in range(NM):
                        P.op("t", lambda t, m=m, dm=dm, yps=yps, e=e: t.matmul(
                            yps,
                            W2RES[:, m * 1024 + dm * 128: m * 1024 + dm * 128 + 128],
                            AR3[:, m * 512:(m + 1) * 512],
                            start=(m == 0 and e != 0), stop=(m == NM - 1)),
                             inc="t" if m == NM - 1 else None)
                    P.wait("v", "t", P.cnt["t"])
                    dstv = VACC[:, dm * 2048 + tc * 512: dm * 2048 + (tc + 1) * 512]
                    P.op("v", lambda v, d=dstv, yps=yps: v.tensor_add(
                        out=d, in0=d, in1=yps), inc="v")
        P.wait("t", "v", P.cnt["v"])

    t_w1 = {}

    def ln2_update(xsrc, xdst):
        ln_stats(VACC, src_is_bf16=True)
        for qc in range(NTC):
            ln_bcast(qc)
            P.wait("v", "t", P.cnt["t"])
            for c in range(NCH):
                i = c * NTC + qc
                xc = XCH[i % 2]
                P.wait("s", f"qxc{i % 2}", P.cnt[f"qxc{i % 2}"])
                P.dma("s", "qv", xc[:], xsrc[ts(c), ts(qc, 512)])
                P.wait("v", "s", P.cnt["s"])
                P.op("v", lambda v, c=c, qc=qc: v.tensor_sub(
                    out=TMPF[:], in0=sl2(VACC, c, qc), in1=PS_E[:]))
                P.op("v", lambda v: v.tensor_mul(
                    out=TMPF[:], in0=TMPF[:], in1=PS_F[:]), inc="v")
                P.wait("s", "v", P.cnt["v"])
                P.op("s", lambda s, c=c: s.activation(
                    out=TMPF[:], in_=TMPF[:], func=AF.Identity,
                    scale=G2[:, c:c + 1], bias=BT2[:, c:c + 1]), inc="s")
                P.wait("v", "s", P.cnt["s"])
                P.wait("v", "qv", P.cnt["qv"])
                P.op("v", lambda v, xc=xc: v.scalar_tensor_tensor(
                    out=xc[:], in0=TMPF[:], scalar=0.5, in1=xc[:],
                    op0=ALU.mult, op1=ALU.add), inc="v")
                P.wait("y", "v", P.cnt["v"])
                P.dma("y", f"qxc{i % 2}", xdst[ts(c), ts(qc, 512)], xc[:])
            P.wait("t", "v", P.cnt["v"])
            P.wait("t", "s", P.cnt["s"])

    def load_x(xsrc):
        for c in range(NCH):
            P.dma("s", "qs", AR1[:, ts(c, 2048)], xsrc[ts(c), 0:2048])
        for e in ("s", "t", "v"):
            P.wait(e, "qs", P.cnt["qs"])

    def ffn_sublayer(xsrc, xdst):
        load_x(xsrc)
        ln_stats(AR1)
        ln_apply(AR1, AR2, AR1, None, None)
        if STAGE == 1:
            return
        attention(fW, PBf, AR1, causal=False, blk="f")
        if state.get("stop"):
            return
        if STAGE == 2:
            return
        ln_stats(AR1)
        ln_apply(AR1, None, AR1, G1, BT1)
        moe()
        ln2_update(xsrc, xdst)

    def attn_sublayer(xsrc, xdst):
        load_x(xsrc)
        ln_stats(AR1)
        ln_apply(None, AR2, AR1, None, None)
        P.wait("g", "v", P.cnt["v"])
        P.wait("g", "s", P.cnt["s"])
        for d in range(4):
            seg = DIAG[:, ts(d, 512)]
            P.op("g", lambda g, s=seg: g.memset(s, 0.0), inc="g")
            P.wait("g", "g", P.cnt["g"])
            P.op("g", lambda g, s=seg, d=d: g.affine_select(
                out=s, in_=s, compare_op=ALU.is_ge, fill=NEGM,
                base=-128 * d, pattern=[[1, 512]], channel_multiplier=-1),
                 inc="g" if d == 3 else None)
        P.wait("t", "g", P.cnt["g"])
        attention(aW, PBa, None, causal=True, x_from=xsrc, x_to=xdst, blk="a")

    phase_consts()
    ffn_sublayer(xT, xbuf)
    if STAGE >= 4:
        attn_sublayer(xbuf, xbuf)
    if STAGE >= 5:
        ffn_sublayer(xbuf, yT)

    if DBG and STAGE == 3:
        for q in ("qy", "qg", "qs", "qv", "qxc0", "qxc1"):
            P.wait("y", q, P.cnt[q])
        P.wait("y", "v", P.cnt["v"])
        P.wait("y", "s", P.cnt["s"])
        P.dma("y", "qy", dbg, xbuf)

    if DBG and STAGE in (1, 2):
        if DBG == "q":
            P.wait("v", "s", P.cnt["s"])
            for c in range(NCH):
                P.op("v", lambda v, c=c: v.tensor_copy(
                    out=AR1[:, ts(c, 2048)], in_=AR3[:, ts(c, 2048)]),
                     inc="v" if c == NCH - 1 else None)
        P.wait("y", "v", P.cnt["v"])
        P.wait("y", "s", P.cnt["s"])
        for c in range(NCH):
            P.dma("y", "qy", dbg[ts(c), 0:2048], AR1[:, ts(c, 2048)])

    if STAGE < 5:
        P.op("g", lambda g: g.memset(XCH[0][:], 0.0), inc="g")
        P.wait("y", "g", P.cnt["g"])
        for c in range(NCH):
            for qc in range(NTC):
                P.dma("y", "qy", yT[ts(c), ts(qc, 512)], XCH[0][:])

    for q in ("qy", "qg", "qs", "qv", "qkv0", "qkv1", "qhb0", "qhb1",
              "qxc0", "qxc1"):
        P.wait("y", q, P.cnt[q])

    nc._dbg_handles = {"AR1": AR1, "AR2": AR2, "AR3": AR3, "KT0": KT0,
                       "KT1": KT1, "VB0": VB0, "VB1": VB1, "PT0": PT0,
                       "PT1": PT1, "OH4": OH4, "VACC": VACC, "W2RES": W2RES,
                       "IROW": IROW, "ROWA": ROWA, "IDX4": IDX4}

    with nc.Block() as block:
        @block.tensor
        def _(t):
            for f in P.ops["t"]:
                f(t)

        @block.scalar
        def _(s):
            for f in P.ops["s"]:
                f(s)

        @block.vector
        def _(v):
            for f in P.ops["v"]:
                f(v)

        @block.gpsimd
        def _(g):
            for f in P.ops["g"]:
                f(g)

        @block.sync
        def _(y):
            for f in P.ops["y"]:
                f(y)

    return nc


def _sel4():
    s = np.zeros((4, 512), dtype=ml_dtypes.bfloat16)
    for e in range(4):
        s[e, e * 128:(e + 1) * 128] = 1
    return s


def pack_w(w):
    t = w.reshape(8, 128, 8, 128).transpose(2, 0, 1, 3)
    t = t.reshape(8, 2, 4, 128, 128).transpose(0, 1, 3, 2, 4)
    return np.ascontiguousarray(t.reshape(8, 2, 128, 512))


def kernel(**inp):
    nc = build_nc()
    g = lambda n: np.asarray(inp[n], np.float32)

    w1 = g("L0_ffn_W1")
    w1t = np.stack([
        np.stack([pack_w(w1[e, :, m * 128:(m + 1) * 128].reshape(1024, 128)
                         .repeat(1, axis=0)) for m in range(0)])
        for e in range(0)]) if False else None
    w1p = np.empty((NE, NM, 2, 128, 512), np.float32)
    for e in range(NE):
        t = w1[e].reshape(8, 128, 32, 128).transpose(2, 0, 1, 3)
        t = t.reshape(32, 2, 4, 128, 128).transpose(0, 1, 3, 2, 4)
        w1p[e] = t.reshape(32, 2, 128, 512)

    def vec128(v):
        return np.ascontiguousarray(v.reshape(NCH, 128).T)

    shared = {
        "w1t": np.ascontiguousarray(w1p),
        "w2r": np.ascontiguousarray(g("L0_ffn_W2")),
        "b1r": np.ascontiguousarray(
            g("L0_ffn_b1").reshape(NE, NM, 128).transpose(2, 0, 1).reshape(128, 128)),
        "b2r": np.ascontiguousarray(g("L0_ffn_b2")),
        "rw": vec128(g("L0_ffn_router_w").reshape(D)),
        "rbias": g("L0_ffn_router_b").reshape(1, 1),
        "ln1g": vec128(g("L0_ffn_ln1_g")),
        "ln1b": vec128(g("L0_ffn_ln1_b")),
        "ln2g": vec128(g("L0_ffn_ln2_g")),
        "ln2b": vec128(g("L0_ffn_ln2_b")),
        "selin": _sel4(),
        "e4in": np.arange(4, dtype=np.float32).reshape(4, 1),
    }
    for p, wn, bn in (("q", "Wq", "bq"), ("k", "Wk", "bk"),
                      ("v", "Wv", "bv"), ("o", "Wo", "bo")):
        shared[f"fW{p}"] = pack_w(g(f"L0_ffn_{wn}"))
        shared[f"fb{p}"] = vec128(g(f"L0_ffn_{bn}"))
        shared[f"aW{p}"] = pack_w(np.ascontiguousarray(g(f"L0_attn_{wn}").T))
        shared[f"ab{p}"] = vec128(g(f"L0_attn_{bn}"))

    x = g("L0_attn_x")
    eye = np.eye(128, dtype=ml_dtypes.bfloat16)
    zer = np.zeros((128, 128), dtype=ml_dtypes.bfloat16)
    in_maps = []
    for c in range(8):
        b, h = c // 2, c % 2
        m = dict(shared)
        m["xT"] = np.ascontiguousarray(x[b, h * T:(h + 1) * T, :].T)
        m["sI"] = eye if h == 0 else zer
        m["sJ"] = zer if h == 0 else eye
        in_maps.append(m)

    res = run_bass_kernel_spmd(nc, in_maps, core_ids=list(range(8)))
    out = np.empty((B, S, D), np.float32)
    for c in range(8):
        b, h = c // 2, c % 2
        out[b, h * T:(h + 1) * T, :] = res.results[c]["yT"].T
    return out


# version 11
# speedup vs baseline: 1.0106x; 1.0106x over previous
import numpy as np
import ml_dtypes

import concourse.bass as bass
import concourse.mybir as mybir
from concourse.bass_utils import run_bass_kernel_spmd

F32 = mybir.dt.float32
BF16 = mybir.dt.bfloat16
I32 = mybir.dt.int32
AF = mybir.ActivationFunctionType
ALU = mybir.AluOpType

B, S, D, H, HD = 4, 4096, 1024, 16, 64
DFF, NE = 4096, 4
T = 2048
NCH = 8
NTC = 4
NQB = 4
NKC = 32
NPAIR = 8
NM = 32
EPS = 1e-5
NEGM = -9.0e15

STAGE = 5
DBG = False
PARANOID = 0


def ts(i, n=128):
    return slice(i * n, (i + 1) * n)


class Prog:
    def __init__(self, nc):
        self.nc = nc
        self.ops = {k: [] for k in ("t", "s", "v", "g", "y")}
        self.sems = {}
        self.cnt = {}

    def sem(self, name):
        h = self.nc.semaphore(name).__enter__()
        self.sems[name] = h
        self.cnt[name] = 0
        return h

    def emit(self, eng, fn):
        self.ops[eng].append(fn)

    def wait(self, eng, sem, val):
        if val <= 0:
            return
        h = self.sems[sem]
        self.emit(eng, lambda e, h=h, v=val: e.wait_ge(h, v))

    def op(self, eng, fn, inc=None, k=1):
        if inc is None:
            self.emit(eng, fn)
            return None
        h = self.sems[inc]
        self.cnt[inc] += k
        self.emit(eng, lambda e, h=h, fn=fn, k=k: fn(e).then_inc(h, k))
        return self.cnt[inc]

    def dma(self, eng, sem, out, in_):
        return self.op(eng, lambda e, o=out, i=in_: e.dma_start(out=o, in_=i),
                       inc=sem, k=16)


def build_nc():
    nc = bass.Bass("TRN2", target_bir_lowering=False, debug=False,
                   detect_race_conditions=False)

    def din(name, shape, dt=F32):
        return nc.dram_tensor(name, shape, dt, kind="ExternalInput").ap()

    xT = din("xT", [D, T])
    fW = {p: din(f"fW{p}", [NCH, 2, 128, 512]) for p in "qkvo"}
    aW = {p: din(f"aW{p}", [NCH, 2, 128, 512]) for p in "qkvo"}
    fb = {p: din(f"fb{p}", [128, NCH]) for p in "qkvo"}
    ab = {p: din(f"ab{p}", [128, NCH]) for p in "qkvo"}
    w1t = din("w1t", [NE, NM, 2, 128, 512])
    w2r = din("w2r", [NE, DFF, D])
    b1r = din("b1r", [128, 128])
    b2r = din("b2r", [NE, D])
    rw = din("rw", [128, NCH])
    rbias = din("rbias", [1, 1])
    lng = {n: din(n, [128, NCH]) for n in ("ln1g", "ln1b", "ln2g", "ln2b")}
    sI = din("sI", [128, 128], BF16)
    sJ = din("sJ", [128, 128], BF16)
    selin = din("selin", [4, 512], BF16)
    e4in = din("e4in", [4, 1], F32)

    yT = nc.dram_tensor("yT", [D, T], F32, kind="ExternalOutput").ap()
    dbg = nc.dram_tensor("dbgo", [D, T], F32, kind="ExternalOutput").ap() \
        if DBG else None

    xbuf = nc.dram_tensor("xbuf", [D, T], F32).ap()
    ccKi = [nc.dram_tensor(f"ccKi{p}", [128, T], BF16).ap() for p in range(NPAIR)]
    ccKo = [nc.dram_tensor(f"ccKo{p}", [256, T], BF16).ap() for p in range(NPAIR)]
    ccVi = [nc.dram_tensor(f"ccVi{p}", [16, 128, 128], BF16).ap() for p in range(NPAIR)]
    ccVo = [nc.dram_tensor(f"ccVo{p}", [32, 128, 128], BF16).ap() for p in range(NPAIR)]
    hbounce = nc.dram_tensor("hbounce", [NM, NTC, 128, 512], BF16).ap()

    SB0 = nc.alloc_sbuf_tensor("SBBASE", [128, 32], mybir.dt.uint8)
    base = nc.lookup_mloc(SB0).addr + 32
    cur = [base]

    def at(name, shape, dt, align=64):
        nbytes = int(np.prod(shape[1:])) * mybir.dt.size(dt)
        cur[0] = (cur[0] + align - 1) // align * align
        h = nc.alloc_sbuf_tensor_at(name, shape, dt, offset=cur[0])
        cur[0] += nbytes
        return h

    def alias(name, shape, dt, offset):
        return nc.alloc_sbuf_tensor_at(name, shape, dt, offset=offset)

    a1 = cur[0]
    AR1 = at("AR1", [128, 16384], F32)
    W2RES = alias("W2RES", [128, 32768], BF16, a1)
    AR2 = at("AR2", [128, 16384], BF16)
    AR3 = at("AR3", [128, 16384], BF16)
    a4 = cur[0]
    KT0 = at("KT0", [128, 4096], BF16)
    KT1 = at("KT1", [128, 4096], BF16)
    VB0 = at("VB0", [128, 4096], BF16)
    VB1 = at("VB1", [128, 4096], BF16)
    PT0 = at("PT0", [128, 1024], BF16)
    PT1 = at("PT1", [128, 1024], BF16)
    VTMP = at("VTMP", [128, 2048], BF16)
    VTM = at("VTM", [128, 2048], BF16)
    KST = at("KST", [128, 2048], BF16)
    VACC = alias("VACC", [128, 16384], BF16, a4)

    ws0_off = cur[0]
    WS = [at("WS0", [128, 512], F32), at("WS1", [128, 512], F32)]
    wt0_off = cur[0]
    WT = [at("WT0", [128, 1024], BF16), at("WT1", [128, 1024], BF16)]
    hbs_off = cur[0]
    HBS = [at("HBS0", [128, 512], BF16), at("HBS1", [128, 512], BF16)]
    IDENT = at("IDENT", [128, 128], BF16)
    SIB = at("SIB", [128, 128], BF16)
    SJB = at("SJB", [128, 128], BF16)
    CONSTM = at("CONSTM", [128, 512], BF16)
    DIAG = alias("DIAG", [128, 2048], BF16, a1)
    rowa_off = cur[0]
    ROWA = at("ROWA", [128, 4096], F32)
    IROW = alias("IROW", [128, 2048], I32, rowa_off)
    G1 = at("G1", [128, NCH], F32)
    BT1 = at("BT1", [128, NCH], F32)
    G2 = at("G2", [128, NCH], F32)
    BT2 = at("BT2", [128, NCH], F32)
    PBf = {p: at(f"PBf{p}", [128, NCH], F32) for p in "qkvo"}
    PBa = {p: at(f"PBa{p}", [128, NCH], F32) for p in "qkvo"}
    B1S = at("B1S", [128, 128], F32)
    RWS = at("RWS", [128, NCH], F32)
    RB1 = at("RB1", [1, 1], F32)
    SEL = at("SEL", [4, 512], BF16)
    E4 = at("E4", [4, 1], F32)
    ONE1F = at("ONE1F", [128, 128], F32)
    ONE1B = at("ONE1B", [1, 128], BF16)
    ONEC = at("ONEC", [128, 1], BF16)
    vtmp_off = nc.lookup_mloc(VTMP).addr
    vtm_off = nc.lookup_mloc(VTM).addr
    kst_off = nc.lookup_mloc(KST).addr
    pt0_off = nc.lookup_mloc(PT0).addr
    IDX4 = alias("IDX4", [4, 2048], I32, vtmp_off)
    OH4 = alias("OH4", [4, 2048], BF16, kst_off)
    B2E = alias("B2E", [4, 1024], BF16, pt0_off)
    RECB = alias("RECB", [128, 512], F32, hbs_off)
    XCH = [alias("XCH0", [128, 512], F32, vtmp_off),
           alias("XCH1", [128, 512], F32, vtmp_off + 2048)]
    CAST0 = alias("CAST0", [128, 2048], BF16, ws0_off)
    CAST1 = alias("CAST1", [128, 2048], BF16, wt0_off)
    TMPF = alias("TMPF", [128, 512], F32, hbs_off)

    print('SBUF used:', cur[0] - base, 'limit', 229376 - base)
    assert cur[0] - base <= 213000, f"sbuf over budget: {cur[0] - base}"

    PS_A = nc.psum_tensor("PS_A", [128, 1024], F32).__enter__()
    PS_B = nc.psum_tensor("PS_B", [128, 1024], F32).__enter__()
    PS_C = nc.psum_tensor("PS_C", [128, 512], F32).__enter__()
    PS_D = nc.psum_tensor("PS_D", [128, 512], F32).__enter__()
    PS_E = nc.psum_tensor("PS_E", [128, 512], F32).__enter__()
    PS_F = nc.psum_tensor("PS_F", [128, 512], F32).__enter__()
    STPS = [PS_A, PS_B]
    OTPS = [PS_C, PS_D]

    P = Prog(nc)
    for s in ("qy", "qg", "qs", "qv", "t", "s", "v", "g", "qkv0", "qkv1",
              "qhb0", "qhb1", "qxc0", "qxc1",
              "cc0", "cc1", "cc2", "cc3", "cc4", "cc5", "cc6", "cc7"):
        P.sem(s)

    def sl2(buf, c, qc):
        return buf[:, c * 2048 + qc * 512: c * 2048 + (qc + 1) * 512]

    def sum_slot(qc):
        return (PS_E, PS_F, PS_C, PS_D)[qc][0:1, :]

    def sq_slot(qc):
        return (PS_A.ap()[0:1, 0:512], PS_A.ap()[0:1, 512:1024],
                PS_B.ap()[0:1, 0:512], PS_B.ap()[0:1, 512:1024])[qc]

    def phase_consts():
        P.op("g", lambda g: g.memset(IDENT[:], 0.0), inc="g")
        P.wait("g", "g", P.cnt["g"])
        P.op("g", lambda g: g.affine_select(
            out=IDENT[:], in_=IDENT[:], compare_op=ALU.not_equal, fill=1.0,
            base=0, pattern=[[-1, 128]], channel_multiplier=1))
        P.op("g", lambda g: g.memset(CONSTM[:], NEGM))
        P.op("g", lambda g: g.memset(ONE1F[:], 1.0))
        P.op("g", lambda g: g.memset(ONE1B[:], 1.0))
        P.op("g", lambda g: g.memset(ONEC[:], 1.0), inc="g")
        P.dma("s", "qs", SIB[:], sI)
        P.dma("s", "qs", SJB[:], sJ)
        for dst, src in ((G1, lng["ln1g"]), (BT1, lng["ln1b"]),
                         (G2, lng["ln2g"]), (BT2, lng["ln2b"]),
                         (B1S, b1r), (RWS, rw)):
            P.dma("s", "qs", dst[:], src)
        for p in "qkvo":
            P.dma("s", "qs", PBf[p][:], fb[p])
            P.dma("s", "qs", PBa[p][:], ab[p])
        P.dma("s", "qs", RB1[:], rbias)
        P.dma("s", "qs", SEL[:], selin)
        P.dma("s", "qs", E4[:], e4in)
        for e in ("t", "s", "v", "y"):
            P.wait(e, "qs", P.cnt["qs"])
            P.wait(e, "g", P.cnt["g"])
        P.wait("g", "qs", P.cnt["qs"])

    def ln_stats(src, src_is_bf16=False):
        for c in range(NCH):
            if not src_is_bf16:
                P.wait("s", "t", P.cnt["t"])
                P.op("s", lambda s, c=c: s.activation(
                    out=CAST0[:], in_=src[:, ts(c, 2048)], func=AF.Copy),
                     inc="s")
                P.wait("t", "s", P.cnt["s"])
            for qc in range(NTC):
                first, last = (c == 0), (c == NCH - 1)
                s_ap = sl2(src, c, qc) if src_is_bf16 else CAST0[:, ts(qc, 512)]
                P.op("t", lambda t, qc=qc, f=first, l=last, s_ap=s_ap: t.matmul(
                    sum_slot(qc), ONEC[:], s_ap, start=f, stop=l),
                     inc="t" if qc == NTC - 1 else None)
        P.wait("v", "t", P.cnt["t"])
        for qc in range(NTC):
            mu = ROWA[0:1, ts(qc, 512)]
            P.op("v", lambda v, qc=qc, mu=mu: v.tensor_scalar(
                out=mu, in0=sum_slot(qc), scalar1=1.0 / D,
                scalar2=None, op0=ALU.mult), inc="v" if qc == NTC - 1 else None)
        P.wait("t", "v", P.cnt["v"])
        for c in range(NCH):
            P.wait("s", "t", P.cnt["t"])
            P.op("s", lambda s, c=c: s.activation(
                out=CAST1[:], in_=src[:, ts(c, 2048)], func=AF.Square), inc="s")
            P.wait("t", "s", P.cnt["s"])
            for qc in range(NTC):
                first, last = (c == 0), (c == NCH - 1)
                P.op("t", lambda t, qc=qc, f=first, l=last: t.matmul(
                    sum_slot(qc), ONEC[:], CAST1[:, ts(qc, 512)],
                    start=f, stop=l),
                     inc="t" if qc == NTC - 1 else None)
        P.wait("v", "t", P.cnt["t"])
        for qc in range(NTC):
            mu = ROWA[0:1, ts(qc, 512)]
            rs = ROWA[0:1, 2048 + qc * 512: 2048 + (qc + 1) * 512]
            mu2 = TMPF[0:1, :]
            P.op("v", lambda v, qc=qc, rs=rs: v.tensor_scalar(
                out=rs, in0=sum_slot(qc), scalar1=1.0 / D,
                scalar2=None, op0=ALU.mult))
            P.op("v", lambda v, mu=mu, mu2=mu2: v.tensor_mul(out=mu2, in0=mu, in1=mu))
            P.op("v", lambda v, rs=rs, mu2=mu2: v.tensor_sub(
                out=rs, in0=rs, in1=mu2))
            P.op("v", lambda v, rs=rs: v.tensor_scalar(
                out=rs, in0=rs, scalar1=EPS, scalar2=None, op0=ALU.add))
            P.op("v", lambda v, rs=rs: v.reciprocal(out=rs, in_=rs), inc="v")
        P.wait("s", "v", P.cnt["v"])
        for qc in range(NTC):
            rs = ROWA[0:1, 2048 + qc * 512: 2048 + (qc + 1) * 512]
            P.op("s", lambda s, rs=rs: s.activation(
                out=rs, in_=rs, func=AF.Sqrt), inc="s")
        P.wait("t", "s", P.cnt["s"])

    def ln_bcast(qc):
        P.op("t", lambda t, qc=qc: t.matmul(
            PS_E[:], ONE1F[0:1, :], ROWA[0:1, ts(qc, 512)],
            start=True, stop=True))
        P.op("t", lambda t, qc=qc: t.matmul(
            PS_F[:], ONE1F[0:1, :],
            ROWA[0:1, 2048 + qc * 512: 2048 + (qc + 1) * 512],
            start=True, stop=True), inc="t")

    def ln_apply(dst32, dst16, src, gamma, beta, src_is_bf16=False):
        for qc in range(NTC):
            ln_bcast(qc)
            P.wait("v", "t", P.cnt["t"])
            for c in range(NCH):
                P.wait("v", "s", P.cnt["s"])
                P.op("v", lambda v, c=c, qc=qc: v.tensor_sub(
                    out=TMPF[:], in0=sl2(src, c, qc), in1=PS_E[:]))
                t32 = sl2(dst32, c, qc) if dst32 is not None else None
                t16 = sl2(dst16, c, qc) if dst16 is not None else None
                if gamma is None:
                    if t32 is not None and t16 is not None:
                        P.op("v", lambda v: v.tensor_mul(
                            out=TMPF[:], in0=TMPF[:], in1=PS_F[:]))
                        P.op("v", lambda v, t=t32: v.tensor_copy(
                            out=t, in_=TMPF[:]), inc="v")
                        P.wait("s", "v", P.cnt["v"])
                        P.op("s", lambda s, t=t16: s.activation(
                            out=t, in_=TMPF[:], func=AF.Copy), inc="s")
                    elif t16 is not None:
                        P.op("v", lambda v, t=t16: v.tensor_mul(
                            out=t, in0=TMPF[:], in1=PS_F[:]), inc="v")
                    else:
                        P.op("v", lambda v, t=t32: v.tensor_mul(
                            out=t, in0=TMPF[:], in1=PS_F[:]), inc="v")
                else:
                    P.op("v", lambda v: v.tensor_mul(
                        out=TMPF[:], in0=TMPF[:], in1=PS_F[:]), inc="v")
                    P.wait("s", "v", P.cnt["v"])
                    gm, bt = gamma[:, c:c + 1], beta[:, c:c + 1]
                    if t32 is not None:
                        P.op("s", lambda s, t=t32, gm=gm, bt=bt: s.activation(
                            out=t, in_=TMPF[:], func=AF.Identity, scale=gm,
                            bias=bt), inc="s")
                    if t16 is not None:
                        P.op("s", lambda s, t=t16, gm=gm, bt=bt: s.activation(
                            out=t, in_=TMPF[:], func=AF.Identity, scale=gm,
                            bias=bt), inc="s")
            P.wait("t", "v", P.cnt["v"])
            P.wait("t", "s", P.cnt["s"])

    def projection(Wt, rhs, evac):
        psr = [PS_A.ap()[:, 0:512], PS_A.ap()[:, 512:1024],
               PS_B.ap()[:, 0:512], PS_B.ap()[:, 512:1024]]
        t_entry = P.cnt["t"]
        t_after_m = {}
        v_cast = {}
        v_entry = P.cnt["v"]

        def stream(mm_):
            for half in range(2):
                ws = WS[half]
                P.wait("y", "v", v_cast.get((mm_ - 1, half), v_entry))
                P.dma("y", "qy", ws[:], Wt[mm_, half])
                P.wait("v", "qy", P.cnt["qy"])
                P.wait("v", "t", t_after_m[mm_ - 2] if mm_ >= 2 else t_entry)
                v_cast[(mm_, half)] = P.op(
                    "v", lambda v, w2=WT[mm_ % 2], ws=ws, h=half: v.tensor_copy(
                        out=w2[:, ts(h, 512)], in_=ws[:]), inc="v")

        stream(0)
        for m in range(NCH):
            wt = WT[m % 2]
            if m + 1 < NCH:
                stream(m + 1)
            P.wait("t", "v", v_cast[(m, 1)])
            for tc in range(NTC):
                ps = psr[tc]
                for k in range(NCH):
                    P.op("t", lambda t, k=k, tc=tc, wt=wt, ps=ps: t.matmul(
                        ps, wt[:, ts(k)], sl2(rhs, k, tc),
                        start=(k == 0), stop=(k == NCH - 1)),
                         inc="t" if k == NCH - 1 else None)
                evac(m, tc, ps)
            t_after_m[m] = P.cnt["t"]
            if PARANOID == 1:
                for e in ("t", "s", "v", "g", "y"):
                    for sm in ("t", "s", "v", "qy", "qg", "qs", "qv"):
                        P.wait(e, sm, P.cnt[sm])
            elif PARANOID == 2:
                for sm in ("t", "s", "v"):
                    P.wait("y", sm, P.cnt[sm])
            elif PARANOID == 3:
                for sm in ("qy", "qg", "qs", "qv"):
                    P.wait("t", sm, P.cnt[sm])
            elif PARANOID == 4:
                for sm in ("t", "s", "v", "qy"):
                    P.wait("g", sm, P.cnt[sm])
            elif PARANOID == 5:
                for sm in ("qy", "qg"):
                    P.wait("s", sm, P.cnt[sm])

    state = {"aid": 0}

    def attention(Wt, bias, u_dst, causal, x_from=None, x_to=None, blk="f"):
        aid = state["aid"]
        state["aid"] += 1
        sc = 1.0 / np.sqrt(HD)

        def evac_q(m, tc, ps):
            P.wait("s", "t", P.cnt["t"])
            P.op("s", lambda s, m=m, tc=tc, ps=ps: s.activation(
                out=sl2(AR3, m, tc), in_=ps, func=AF.Identity,
                bias=bias["q"][:, m:m + 1]), inc="s")
            P.wait("t", "s", P.cnt["s"])

        def evac_k(m, tc, ps):
            P.wait("s", "t", P.cnt["t"])
            P.wait("s", "qg", P.cnt["qg"])
            P.op("s", lambda s, tc=tc, ps=ps, m=m: s.activation(
                out=KST[:, ts(tc, 512)], in_=ps, func=AF.Identity,
                bias=bias["k"][:, m:m + 1]), inc="s")
            P.wait("g", "s", P.cnt["s"])
            P.dma("g", "qg", ccKi[m][:, ts(tc, 512)], KST[:, ts(tc, 512)])
            P.wait("t", "s", P.cnt["s"])

        def evac_v(m, tc, ps):
            P.wait("s", "t", P.cnt["t"])
            P.wait("s", "qy", P.cnt["qy"])
            P.op("s", lambda s, tc=tc, ps=ps, m=m: s.activation(
                out=VTMP[:, ts(tc, 512)], in_=ps, func=AF.Identity,
                bias=bias["v"][:, m:m + 1]), inc="s")
            P.wait("y", "s", P.cnt["s"])
            P.wait("y", "qg", P.cnt["qg"])
            for i in range(4):
                kt = tc * 4 + i
                P.op("y", lambda y, tc=tc, i=i, kt=kt: y.dma_start_transpose(
                    out=VTM[:, ts(kt % 16)],
                    in_=VTMP[:, tc * 512 + i * 128: tc * 512 + (i + 1) * 128]),
                     inc="qy", k=16)
            P.wait("g", "qy", P.cnt["qy"])
            for i in range(4):
                kt = tc * 4 + i
                P.dma("g", "qg", ccVi[m][kt], VTM[:, ts(kt % 16)])
            P.wait("t", "s", P.cnt["s"])

        projection(Wt["q"], AR2, evac_q)
        projection(Wt["k"], AR2, evac_k)
        projection(Wt["v"], AR2, evac_v)
        if DBG == "q":
            state["stop"] = True
            return

        P.wait("g", "qg", P.cnt["qg"])
        if DBG == "cki":
            P.dma("g", "qkv0", KT0[:, 0:2048], ccKi[1])
            P.wait("v", "qkv0", P.cnt["qkv0"])
            P.op("v", lambda v: v.tensor_copy(out=AR1[:, 0:2048],
                                              in_=KT0[:, 0:2048]), inc="v")
            state["stop"] = True
            return
        P.wait("g", "qkv0", P.cnt["qkv0"])
        P.wait("g", "qkv1", P.cnt["qkv1"])
        rg = [[0, 1], [2, 3], [4, 5], [6, 7]]
        for p in range(NPAIR):
            P.op("g", lambda g, p=p: g.collective_compute(
                "AllGather", ALU.bypass, replica_groups=rg,
                ins=[ccKi[p]], outs=[ccKo[p]]), inc=f"cc{p}")
            P.op("g", lambda g, p=p: g.collective_compute(
                "AllGather", ALU.bypass, replica_groups=rg,
                ins=[ccVi[p]], outs=[ccVo[p]]), inc=f"cc{p}")

        t_after_pair = {}
        git = [0]
        for p in range(NPAIR):
            KT = (KT0, KT1)[p % 2]
            VB = (VB0, VB1)[p % 2]
            P.wait("g", f"cc{p}", P.cnt[f"cc{p}"])
            if p >= 2:
                P.wait("g", "t", t_after_pair[p - 2])
            kvs = f"qkv{p % 2}"
            P.dma("g", kvs, KT[:, 0:2048], ccKo[p][0:128, :])
            P.dma("g", kvs, KT[:, 2048:4096], ccKo[p][128:256, :])
            vbv = VB.ap().rearrange("p (t f) -> p t f", f=128)
            srcv = ccVo[p].rearrange("t p f -> p t f")
            P.dma("g", kvs, vbv, srcv)
            need_kv = P.cnt[kvs]

            for qb in range(NQB):
                for kc in range(NKC):
                    it = qb * NKC + kc
                    stp = STPS[git[0] % 2]
                    if git[0] >= 2 or state["aid"] > 1:
                        P.wait("t", "s", P.cnt["s"])
                    git[0] += 1
                    if it == 0:
                        P.wait("t", kvs, need_kv)
                    for hh in range(2):
                        lhs = KT[64 * hh:64 * (hh + 1), ts(kc)]
                        rhsq = AR3[64 * hh:64 * (hh + 1) if hh == 0 else 128,
                                   p * 2048 + qb * 512: p * 2048 + (qb + 1) * 512]
                        psl = stp.ap()[:, 512 * hh: 512 * (hh + 1)]
                        mask_mms = []
                        if causal:
                            dd0 = kc - 4 * qb
                            dd1 = kc - 16 - 4 * qb
                            if kc < 16:
                                if 0 <= dd0 < 4:
                                    mask_mms = [(SIB, DIAG[:, ts(dd0, 512)])]
                                elif dd0 >= 4:
                                    mask_mms = [(SIB, CONSTM[:])]
                            else:
                                if dd1 < 0:
                                    mask_mms = [(SIB, CONSTM[:])]
                                elif dd1 < 4:
                                    mask_mms = [(SJB, DIAG[:, ts(dd1, 512)]),
                                                (SIB, CONSTM[:])]
                                else:
                                    mask_mms = [(IDENT, CONSTM[:])]
                        one_mm = not mask_mms
                        P.op("t", lambda t, l=lhs, r=rhsq, ps=psl, hh=hh,
                             om=one_mm: t.matmul(
                                 ps, l, r, start=True, stop=om,
                                 tile_position=(64 * hh, 0)))
                        for j, (ml, mr) in enumerate(mask_mms):
                            lastm = j == len(mask_mms) - 1
                            P.op("t", lambda t, ps=psl, ml=ml, mr=mr,
                                 lm=lastm: t.matmul(ps, ml[:], mr,
                                                    start=False, stop=lm))
                    last = P.ops["t"].pop()
                    P.cnt["t"] += 1
                    h = P.sems["t"]
                    P.emit("t", lambda t, f=last, h=h: f(t).then_inc(h, 1))
                    tdone = P.cnt["t"]
                    P.wait("s", "t", tdone)
                    PT = (PT0, PT1)[(git[0] - 1) % 2]
                    P.op("s", lambda s, stp=stp, PT=PT: s.activation(
                        out=PT[:], in_=stp.ap()[:, 0:1024], func=AF.Exp,
                        scale=sc), inc="s")
                    P.wait("t", "s", P.cnt["s"])
                    for hh in range(2):
                        lhsv = VB[:, kc * 128 + hh * 64: kc * 128 + (hh + 1) * 64]
                        rr = PT.ap()[:, 512 * hh: 512 * (hh + 1)]
                        P.op("t", lambda t, l=lhsv, r=rr, kc=kc, ot=OTPS[hh]:
                             t.matmul(ot[0:64, :], l, r, start=(kc == 0),
                                      stop=(kc == NKC - 1)))
                        sxp = (PS_E, PS_F)[hh]
                        P.op("t", lambda t, r=rr, kc=kc, sxp=sxp:
                             t.matmul(sxp[64:65, :], ONEC[:], r,
                                      start=(kc == 0), stop=(kc == NKC - 1),
                                      tile_position=(0, 64)),
                             inc="t" if (hh == 1 and kc == NKC - 1) else None)
                P.wait("v", "t", P.cnt["t"])
                r0 = ROWA[0:1, 0:512]
                r1 = ROWA[0:1, 512:1024]
                P.op("v", lambda v, r0=r0: v.reciprocal(out=r0, in_=PS_E[64:65, :]))
                P.op("v", lambda v, r1=r1: v.reciprocal(out=r1, in_=PS_F[64:65, :]),
                     inc="v")
                P.wait("t", "v", P.cnt["v"])
                P.op("t", lambda t, r0=r0: t.matmul(
                    PS_E[:], ONE1F[0:1, :], r0, start=True, stop=True))
                P.op("t", lambda t, r1=r1: t.matmul(
                    PS_F[:], ONE1F[0:1, :], r1, start=True, stop=True),
                     inc="t")
                P.wait("s", "t", P.cnt["t"])
                P.op("s", lambda s: s.copy(out=RECB[0:64, :], in_=PS_C[0:64, :]))
                P.op("s", lambda s: s.copy(out=RECB[64:128, :], in_=PS_D[0:64, :]),
                     inc="s")
                P.wait("v", "s", P.cnt["s"])
                dst = AR2[:, p * 2048 + qb * 512: p * 2048 + (qb + 1) * 512]
                P.op("v", lambda v, dst=dst: v.tensor_mul(
                    out=dst[0:64, :], in0=RECB[0:64, :], in1=PS_E[0:64, :]))
                P.op("v", lambda v, dst=dst: v.tensor_mul(
                    out=dst[64:128, :], in0=RECB[64:128, :], in1=PS_F[64:128, :]),
                     inc="v")
                P.wait("t", "v", P.cnt["v"])
            t_after_pair[p] = P.cnt["t"]
            if DBG == "kv1" and p == 1:
                P.wait("v", "qkv1", P.cnt["qkv1"])
                P.op("v", lambda v: v.tensor_copy(out=AR1[:, 0:4096], in_=KT1[:]))
                P.op("v", lambda v: v.tensor_copy(out=AR1[:, 4096:8192],
                                                  in_=VB1[:]), inc="v")
                state["stop"] = True
                return
            if DBG == "kv" and p == 0:
                P.wait("v", "qkv0", P.cnt["qkv0"])
                P.op("v", lambda v: v.tensor_copy(out=AR1[:, 0:4096], in_=KT0[:]))
                P.op("v", lambda v: v.tensor_copy(out=AR1[:, 4096:8192],
                                                  in_=VB0[:]), inc="v")
                state["stop"] = True
                return
        if DBG == "ot":
            P.wait("v", "t", P.cnt["t"])
            for c in range(NCH):
                P.op("v", lambda v, c=c: v.tensor_copy(
                    out=AR1[:, ts(c, 2048)], in_=AR2[:, ts(c, 2048)]),
                     inc="v" if c == NCH - 1 else None)
            state["stop"] = True
            return

        def evac_o(m, tc, ps):
            P.wait("v", "t", P.cnt["t"])
            if blk == "f":
                dstc = sl2(u_dst, m, tc)
                P.op("v", lambda v, d=dstc, ps=ps, m=m: v.scalar_tensor_tensor(
                    out=d, in0=ps, scalar=bias["o"][:, m:m + 1], in1=d,
                    op0=ALU.add, op1=ALU.add), inc="v")
            else:
                i = m * NTC + tc
                xc = XCH[i % 2]
                P.wait("s", f"qxc{i % 2}", P.cnt[f"qxc{i % 2}"])
                P.dma("s", "qv", xc[:], x_from[ts(m), ts(tc, 512)])
                P.wait("v", "qv", P.cnt["qv"])
                P.op("v", lambda v, xc=xc, ps=ps, m=m: v.scalar_tensor_tensor(
                    out=xc[:], in0=ps, scalar=bias["o"][:, m:m + 1], in1=xc[:],
                    op0=ALU.add, op1=ALU.add), inc="v")
                P.wait("y", "v", P.cnt["v"])
                P.dma("y", f"qxc{i % 2}", x_to[ts(m), ts(tc, 512)], xc[:])
            P.wait("t", "v", P.cnt["v"])

        projection(Wt["o"], AR2, evac_o)

    def moe():
        P.wait("t", "s", P.cnt["s"])
        for qc in range(NTC):
            for k in range(NCH):
                P.op("t", lambda t, qc=qc, k=k: t.matmul(
                    sum_slot(qc), RWS[:, k:k + 1], sl2(AR1, k, qc),
                    start=(k == 0), stop=(k == NCH - 1)),
                     inc="t" if k == NCH - 1 else None)
        P.wait("v", "t", P.cnt["t"])
        for qc in range(NTC):
            P.op("v", lambda v, qc=qc: v.tensor_copy(
                out=ROWA[0:1, ts(qc, 512)], in_=sum_slot(qc)))
        krow = ROWA[0:1, 0:2048]
        P.op("v", lambda v, krow=krow: v.tensor_scalar(
            out=krow, in0=krow, scalar1=RB1[0:1, 0:1], scalar2=None,
            op0=ALU.add))
        irow = IDX4[0:1, :]
        frow = ROWA[0:1, 2048:4096]
        adj = OH4[0:1, :]
        P.op("v", lambda v, krow=krow, irow=irow: v.tensor_copy(
            out=irow, in_=krow))
        P.op("v", lambda v, irow=irow, frow=frow: v.tensor_copy(
            out=frow, in_=irow))
        P.op("v", lambda v, krow=krow, frow=frow, adj=adj: v.tensor_tensor(
            out=adj, in0=frow, in1=krow, op=ALU.is_gt))
        P.op("v", lambda v, frow=frow, adj=adj: v.tensor_tensor(
            out=frow, in0=frow, in1=adj, op=ALU.subtract))
        P.op("v", lambda v, irow=irow, frow=frow: v.tensor_copy(
            out=irow, in_=frow))
        P.op("v", lambda v, irow=irow: v.tensor_scalar(
            out=irow, in0=irow, scalar1=3,
            scalar2=None, op0=ALU.bitwise_and), inc="v")
        P.wait("g", "v", P.cnt["v"])
        for e in range(1, NE):
            P.dma("g", "qg", IDX4[e:e + 1, :], IDX4[0:1, :])
        P.wait("v", "qg", P.cnt["qg"])
        P.op("v", lambda v: v.tensor_scalar(
            out=OH4[:], in0=IDX4[:], scalar1=E4[:, 0:1], scalar2=None,
            op0=ALU.is_equal), inc="v")
        P.wait("g", "v", P.cnt["v"])
        P.dma("g", "qg", B2E[:], b2r)

        for c in range(NCH):
            P.op("s", lambda s, c=c: s.activation(
                out=AR2[:, ts(c, 2048)], in_=AR1[:, ts(c, 2048)],
                func=AF.Copy), inc="s" if c == NCH - 1 else None)
        P.wait("v", "s", P.cnt["s"])
        for c in range(NCH):
            P.op("v", lambda v, c=c: v.tensor_copy(
                out=VACC[:, ts(c, 2048)], in_=AR2[:, ts(c, 2048)]),
                 inc="v" if c == NCH - 1 else None)
        vacc_ready = P.cnt["v"]
        xn_ready = P.cnt["s"]

        t_moe_entry = P.cnt["t"]
        for e in range(NE):
            P.wait("v", "t", t_moe_entry)
            for m in range(NM):
                for hh in range(2):
                    ws = WS[hh]
                    P.wait("y", "v", P.cnt["v"])
                    P.dma("y", "qy", ws[:], w2r[e, ts(m), ts(hh, 512)])
                    P.wait("v", "qy", P.cnt["qy"])
                    P.op("v", lambda v, m=m, hh=hh, ws=ws: v.tensor_copy(
                        out=W2RES[:, m * 1024 + hh * 512: m * 1024 + (hh + 1) * 512],
                        in_=ws[:]), inc="v")
            for qc in range(NTC):
                P.wait("t", "v", P.cnt["v"])
                P.op("t", lambda t, e=e, qc=qc: t.matmul(
                    PS_E[:], SEL[:, ts(e)], OH4[:, ts(qc, 512)],
                    start=True, stop=True), inc="t")
                P.wait("v", "t", P.cnt["t"])
                for k in range(NCH):
                    P.op("v", lambda v, k=k, qc=qc: v.tensor_mul(
                        out=sl2(AR3, k, qc), in0=sl2(AR2, k, qc), in1=PS_E[:]),
                         inc="v" if k == NCH - 1 else None)
            P.wait("t", "v", P.cnt["v"])

            t_p1entry = P.cnt["t"]
            v1_cast = {}
            v1_entry = P.cnt["v"]

            def stream1(mm_, e=e):
                for hh in range(2):
                    ws = WS[hh]
                    P.wait("y", "v", v1_cast.get((mm_ - 1, hh), v1_entry))
                    P.dma("y", "qy", ws[:], w1t[e, mm_, hh])
                    P.wait("v", "qy", P.cnt["qy"])
                    P.wait("v", "t", t_w1[mm_ - 2] if mm_ >= 2 else t_p1entry)
                    v1_cast[(mm_, hh)] = P.op(
                        "v", lambda v, w2=WT[mm_ % 2], ws=ws, hh=hh:
                        v.tensor_copy(out=w2[:, ts(hh, 512)], in_=ws[:]),
                        inc="v")

            stream1(0)
            for m in range(NM):
                wt = WT[m % 2]
                if m + 1 < NM:
                    stream1(m + 1)
                P.wait("t", "v", v1_cast[(m, 1)])
                for tc in range(NTC):
                    hps = OTPS[tc % 2]
                    P.wait("t", "s", P.cnt["s"])
                    for k in range(NCH):
                        P.op("t", lambda t, k=k, tc=tc, wt=wt, hps=hps: t.matmul(
                            hps[:], wt[:, ts(k)], sl2(AR3, k, tc),
                            start=(k == 0), stop=(k == NCH - 1)),
                             inc="t" if k == NCH - 1 else None)
                    P.wait("s", "t", P.cnt["t"])
                    P.wait("s", f"qhb{tc % 2}", P.cnt[f"qhb{tc % 2}"])
                    hst = HBS[tc % 2]
                    b1c = B1S[:, e * NM + m: e * NM + m + 1]
                    P.op("s", lambda s, hps=hps, hst=hst, b1c=b1c: s.activation(
                        out=hst[:], in_=hps[:], func=AF.Relu, bias=b1c),
                         inc="s")
                    P.wait("g", "s", P.cnt["s"])
                    P.dma("g", f"qhb{tc % 2}", hbounce[m, tc], hst[:])
                t_w1[m] = P.cnt["t"]

            P.wait("g", "qhb0", P.cnt["qhb0"])
            P.wait("g", "qhb1", P.cnt["qhb1"])
            for tc in range(NTC):
                P.wait("g", "t", P.cnt["t"])
                src = hbounce[:, tc].rearrange("m p n -> p m n")
                dst = AR3.ap().rearrange("p (m n) -> p m n", n=512)[:, 0:NM]
                P.dma("g", "qg", dst, src)
                P.wait("t", "qg", P.cnt["qg"])
                for dm in range(NCH):
                    yps = PS_A.ap()[:, 0:512] if dm % 2 == 0 \
                        else PS_A.ap()[:, 512:1024]
                    if dm >= 2:
                        P.wait("t", "v", P.cnt["v"])
                    if e == 0:
                        P.op("t", lambda t, dm=dm, tc=tc, yps=yps: t.matmul(
                            yps, B2E[:, ts(dm)], OH4[:, ts(tc, 512)],
                            start=True, stop=False))
                    for m in range(NM):
                        P.op("t", lambda t, m=m, dm=dm, yps=yps, e=e: t.matmul(
                            yps,
                            W2RES[:, m * 1024 + dm * 128: m * 1024 + dm * 128 + 128],
                            AR3[:, m * 512:(m + 1) * 512],
                            start=(m == 0 and e != 0), stop=(m == NM - 1)),
                             inc="t" if m == NM - 1 else None)
                    P.wait("v", "t", P.cnt["t"])
                    dstv = VACC[:, dm * 2048 + tc * 512: dm * 2048 + (tc + 1) * 512]
                    P.op("v", lambda v, d=dstv, yps=yps: v.tensor_add(
                        out=d, in0=d, in1=yps), inc="v")
        P.wait("t", "v", P.cnt["v"])

    t_w1 = {}

    def ln2_update(xsrc, xdst):
        ln_stats(VACC, src_is_bf16=True)
        for qc in range(NTC):
            ln_bcast(qc)
            P.wait("v", "t", P.cnt["t"])
            for c in range(NCH):
                i = c * NTC + qc
                xc = XCH[i % 2]
                P.wait("s", f"qxc{i % 2}", P.cnt[f"qxc{i % 2}"])
                P.dma("s", "qv", xc[:], xsrc[ts(c), ts(qc, 512)])
                P.wait("v", "s", P.cnt["s"])
                P.op("v", lambda v, c=c, qc=qc: v.tensor_sub(
                    out=TMPF[:], in0=sl2(VACC, c, qc), in1=PS_E[:]))
                P.op("v", lambda v: v.tensor_mul(
                    out=TMPF[:], in0=TMPF[:], in1=PS_F[:]), inc="v")
                P.wait("s", "v", P.cnt["v"])
                P.op("s", lambda s, c=c: s.activation(
                    out=TMPF[:], in_=TMPF[:], func=AF.Identity,
                    scale=G2[:, c:c + 1], bias=BT2[:, c:c + 1]), inc="s")
                P.wait("v", "s", P.cnt["s"])
                P.wait("v", "qv", P.cnt["qv"])
                P.op("v", lambda v, xc=xc: v.scalar_tensor_tensor(
                    out=xc[:], in0=TMPF[:], scalar=0.5, in1=xc[:],
                    op0=ALU.mult, op1=ALU.add), inc="v")
                P.wait("y", "v", P.cnt["v"])
                P.dma("y", f"qxc{i % 2}", xdst[ts(c), ts(qc, 512)], xc[:])
            P.wait("t", "v", P.cnt["v"])
            P.wait("t", "s", P.cnt["s"])

    def load_x(xsrc):
        for c in range(NCH):
            P.dma("s", "qs", AR1[:, ts(c, 2048)], xsrc[ts(c), 0:2048])
        for e in ("s", "t", "v"):
            P.wait(e, "qs", P.cnt["qs"])

    def ffn_sublayer(xsrc, xdst):
        load_x(xsrc)
        ln_stats(AR1)
        ln_apply(AR1, AR2, AR1, None, None)
        if STAGE == 1:
            return
        attention(fW, PBf, AR1, causal=False, blk="f")
        if state.get("stop"):
            return
        if STAGE == 2:
            return
        ln_stats(AR1)
        ln_apply(AR1, None, AR1, G1, BT1)
        moe()
        ln2_update(xsrc, xdst)

    def attn_sublayer(xsrc, xdst):
        load_x(xsrc)
        ln_stats(AR1)
        ln_apply(None, AR2, AR1, None, None)
        P.wait("g", "v", P.cnt["v"])
        P.wait("g", "s", P.cnt["s"])
        for d in range(4):
            seg = DIAG[:, ts(d, 512)]
            P.op("g", lambda g, s=seg: g.memset(s, 0.0), inc="g")
            P.wait("g", "g", P.cnt["g"])
            P.op("g", lambda g, s=seg, d=d: g.affine_select(
                out=s, in_=s, compare_op=ALU.is_ge, fill=NEGM,
                base=-128 * d, pattern=[[1, 512]], channel_multiplier=-1),
                 inc="g" if d == 3 else None)
        P.wait("t", "g", P.cnt["g"])
        attention(aW, PBa, None, causal=True, x_from=xsrc, x_to=xdst, blk="a")

    phase_consts()
    ffn_sublayer(xT, xbuf)
    if STAGE >= 4:
        attn_sublayer(xbuf, xbuf)
    if STAGE >= 5:
        ffn_sublayer(xbuf, yT)

    if DBG and STAGE == 3:
        for q in ("qy", "qg", "qs", "qv", "qxc0", "qxc1"):
            P.wait("y", q, P.cnt[q])
        P.wait("y", "v", P.cnt["v"])
        P.wait("y", "s", P.cnt["s"])
        P.dma("y", "qy", dbg, xbuf)

    if DBG and STAGE in (1, 2):
        if DBG == "q":
            P.wait("v", "s", P.cnt["s"])
            for c in range(NCH):
                P.op("v", lambda v, c=c: v.tensor_copy(
                    out=AR1[:, ts(c, 2048)], in_=AR3[:, ts(c, 2048)]),
                     inc="v" if c == NCH - 1 else None)
        P.wait("y", "v", P.cnt["v"])
        P.wait("y", "s", P.cnt["s"])
        for c in range(NCH):
            P.dma("y", "qy", dbg[ts(c), 0:2048], AR1[:, ts(c, 2048)])

    if STAGE < 5:
        P.op("g", lambda g: g.memset(XCH[0][:], 0.0), inc="g")
        P.wait("y", "g", P.cnt["g"])
        for c in range(NCH):
            for qc in range(NTC):
                P.dma("y", "qy", yT[ts(c), ts(qc, 512)], XCH[0][:])

    for q in ("qy", "qg", "qs", "qv", "qkv0", "qkv1", "qhb0", "qhb1",
              "qxc0", "qxc1"):
        P.wait("y", q, P.cnt[q])

    nc._dbg_handles = {"AR1": AR1, "AR2": AR2, "AR3": AR3, "KT0": KT0,
                       "KT1": KT1, "VB0": VB0, "VB1": VB1, "PT0": PT0,
                       "PT1": PT1, "OH4": OH4, "VACC": VACC, "W2RES": W2RES,
                       "IROW": IROW, "ROWA": ROWA, "IDX4": IDX4}

    with nc.Block() as block:
        @block.tensor
        def _(t):
            for f in P.ops["t"]:
                f(t)

        @block.scalar
        def _(s):
            for f in P.ops["s"]:
                f(s)

        @block.vector
        def _(v):
            for f in P.ops["v"]:
                f(v)

        @block.gpsimd
        def _(g):
            for f in P.ops["g"]:
                f(g)

        @block.sync
        def _(y):
            for f in P.ops["y"]:
                f(y)

    return nc


def _sel4():
    s = np.zeros((4, 512), dtype=ml_dtypes.bfloat16)
    for e in range(4):
        s[e, e * 128:(e + 1) * 128] = 1
    return s


def pack_w(w):
    t = w.reshape(8, 128, 8, 128).transpose(2, 0, 1, 3)
    t = t.reshape(8, 2, 4, 128, 128).transpose(0, 1, 3, 2, 4)
    return np.ascontiguousarray(t.reshape(8, 2, 128, 512))


def kernel(**inp):
    nc = build_nc()
    g = lambda n: np.asarray(inp[n], np.float32)

    w1 = g("L0_ffn_W1")
    w1t = np.stack([
        np.stack([pack_w(w1[e, :, m * 128:(m + 1) * 128].reshape(1024, 128)
                         .repeat(1, axis=0)) for m in range(0)])
        for e in range(0)]) if False else None
    w1p = np.empty((NE, NM, 2, 128, 512), np.float32)
    for e in range(NE):
        t = w1[e].reshape(8, 128, 32, 128).transpose(2, 0, 1, 3)
        t = t.reshape(32, 2, 4, 128, 128).transpose(0, 1, 3, 2, 4)
        w1p[e] = t.reshape(32, 2, 128, 512)

    def vec128(v):
        return np.ascontiguousarray(v.reshape(NCH, 128).T)

    shared = {
        "w1t": np.ascontiguousarray(w1p),
        "w2r": np.ascontiguousarray(g("L0_ffn_W2")),
        "b1r": np.ascontiguousarray(
            g("L0_ffn_b1").reshape(NE, NM, 128).transpose(2, 0, 1).reshape(128, 128)),
        "b2r": np.ascontiguousarray(g("L0_ffn_b2")),
        "rw": vec128(g("L0_ffn_router_w").reshape(D)),
        "rbias": g("L0_ffn_router_b").reshape(1, 1),
        "ln1g": vec128(g("L0_ffn_ln1_g")),
        "ln1b": vec128(g("L0_ffn_ln1_b")),
        "ln2g": vec128(g("L0_ffn_ln2_g")),
        "ln2b": vec128(g("L0_ffn_ln2_b")),
        "selin": _sel4(),
        "e4in": np.arange(4, dtype=np.float32).reshape(4, 1),
    }
    for p, wn, bn in (("q", "Wq", "bq"), ("k", "Wk", "bk"),
                      ("v", "Wv", "bv"), ("o", "Wo", "bo")):
        shared[f"fW{p}"] = pack_w(g(f"L0_ffn_{wn}"))
        shared[f"fb{p}"] = vec128(g(f"L0_ffn_{bn}"))
        shared[f"aW{p}"] = pack_w(np.ascontiguousarray(g(f"L0_attn_{wn}").T))
        shared[f"ab{p}"] = vec128(g(f"L0_attn_{bn}"))

    x = g("L0_attn_x")
    eye = np.eye(128, dtype=ml_dtypes.bfloat16)
    zer = np.zeros((128, 128), dtype=ml_dtypes.bfloat16)
    in_maps = []
    for c in range(8):
        b, h = c // 2, c % 2
        m = dict(shared)
        m["xT"] = np.ascontiguousarray(x[b, h * T:(h + 1) * T, :].T)
        m["sI"] = eye if h == 0 else zer
        m["sJ"] = zer if h == 0 else eye
        in_maps.append(m)

    res = run_bass_kernel_spmd(nc, in_maps, core_ids=list(range(8)))
    out = np.empty((B, S, D), np.float32)
    for c in range(8):
        b, h = c // 2, c % 2
        out[b, h * T:(h + 1) * T, :] = res.results[c]["yT"].T
    return out
